# version 02
# baseline (speedup 1.0000x reference)
import numpy as np
import ml_dtypes
from contextlib import ExitStack

import concourse.bass as bass
import concourse.mybir as mybir
from concourse.bass_utils import run_bass_kernel_spmd

F32 = mybir.dt.float32
BF16 = mybir.dt.bfloat16
I32 = mybir.dt.int32
ALU = mybir.AluOpType
ACT = mybir.ActivationFunctionType
AX = mybir.AxisListType.X

S = 2048
D = 1024
INNER = 4096
NT = 16
NG = 4
DC = 8
IC = 32
EPS = 1e-5
INV127 = 1.0 / 127.0


class Sem:
    def __init__(self, ctx, nc, name):
        self.h = ctx.enter_context(nc.semaphore(name))
        self.n = 0

    def inc(self, k=1):
        self.n += k
        return self.n


def build_nc(n_sublayers=3):
    nc = bass.Bass("TRN2", target_bir_lowering=False, debug=False,
                   detect_race_conditions=False)

    x_d = nc.dram_tensor("x", [S, D], F32, kind="ExternalInput").ap()
    wqt_d = nc.dram_tensor("wqt", [D, D], F32, kind="ExternalInput").ap()
    wkt_d = nc.dram_tensor("wkt", [D, D], F32, kind="ExternalInput").ap()
    wvt_d = nc.dram_tensor("wvt", [D, D], F32, kind="ExternalInput").ap()
    wft_d = nc.dram_tensor("wft", [D, D], F32, kind="ExternalInput").ap()
    bff_d = nc.dram_tensor("bff", [1, D], F32, kind="ExternalInput").ap()
    wgt_d = nc.dram_tensor("wgt", [D, INNER], F32, kind="ExternalInput").ap()
    wut_d = nc.dram_tensor("wut", [D, INNER], F32, kind="ExternalInput").ap()
    wdt_d = nc.dram_tensor("wdt", [INNER, D], F32, kind="ExternalInput").ap()
    onef_d = nc.dram_tensor("onef", [1, 128], F32, kind="ExternalInput").ap()
    onesb_d = nc.dram_tensor("onesb", [128, 1], BF16, kind="ExternalInput").ap()
    e2b_d = nc.dram_tensor("e2b", [128, 2], BF16, kind="ExternalInput").ap()
    out_d = nc.dram_tensor("out", [S, D], F32, kind="ExternalOutput").ap()

    import os as _os
    _dbg = set((_os.environ.get("DBGOUT") or "").split(","))
    _k = lambda n: "ExternalOutput" if (n in _dbg or "all" in _dbg) else "Internal"
    rint_d = nc.dram_tensor("rint", [S, D], BF16, kind=_k("rint")).ap()
    ln2_d = nc.dram_tensor("ln2s", [S, D], BF16, kind=_k("ln2s")).ap()
    guq_d = nc.dram_tensor("guqs", [S, INNER], BF16, kind=_k("guqs")).ap()
    r1_d = nc.dram_tensor("r1s", [S, D], F32, kind=_k("r1s")).ap()
    r2_d = nc.dram_tensor("r2s", [S, D], F32, kind=_k("r2s")).ap()
    rint2_d = nc.dram_tensor("rint2", [S, D], BF16, kind=_k("rint2")).ap()
    guq2_d = nc.dram_tensor("guqs2", [S, INNER], BF16, kind=_k("guqs2")).ap()
    wgq_d = nc.dram_tensor("wgqs", [D, INNER], BF16, kind=_k("wgqs")).ap()
    wuq_d = nc.dram_tensor("wuqs", [D, INNER], BF16, kind=_k("wuqs")).ap()
    wdq_d = nc.dram_tensor("wdqs", [INNER, D], BF16, kind=_k("wdqs")).ap()
    sc128_d = nc.dram_tensor("sc128", [1, 128], F32, kind="Internal").ap()
    sc8_d = nc.dram_tensor("sc8", [1, 8], F32, kind="Internal").ap()
    dbg_d = nc.dram_tensor("dbg", [128, 2048], F32, kind=_k("dbg")).ap()
    scm_d = nc.dram_tensor("scm", [2, 8], F32, kind="Internal").ap()
    scmk_d = nc.dram_tensor("scmk", [2, 8], F32, kind="Internal").ap()
    scm2_d = nc.dram_tensor("scm2", [1, 16], F32, kind="Internal").ap()

    top = ExitStack()
    with top:
        sb = lambda name, shape, dt: top.enter_context(nc.sbuf_tensor(name, shape, dt))
        r_tile = [sb("r_tile0", [128, D], F32)]
        ln_t = sb("ln_t", [128, D], F32)
        ln2_t = [sb(f"ln2_t{i}", [128, D], BF16) for i in range(2)]
        bnb = sb("bnb", [128, 12], F32)
        mv = sb("mv", [128, 2], F32)
        ve_t = sb("ve_t", [128, 1], F32)
        sd_t = sb("sd_t", [128, 1], F32)
        rstd_t = sb("rstd_t", [128, 1], F32)
        nmr_t = sb("nmr_t", [128, 1], F32)
        am_t = sb("am_t", [128, 1], F32)
        mx_t = sb("mx_t", [128, 1], F32)
        iv_t = sb("iv_t", [128, 1], F32)
        asc_t = sb("asc_t", [128, 1], F32)
        a_t = sb("a_t", [128, 512], F32)
        ci_t = sb("ci_t", [128, 512], I32)
        mk_t = sb("mk_t", [128, 512], BF16)
        rq_t = [sb(f"rq_t{i}", [128, D], BF16) for i in range(2)]
        dq_g = sb("dq_g", [128, NT], F32)
        dq_u = sb("dq_u", [128, NT], F32)
        dsc_all = sb("dsc_all", [128, NT], F32)
        meanb = {m: sb(f"meanb_{m}", [128, 1], F32) for m in ("g", "u", "d")}
        wsb = {m: sb(f"wsb_{m}", [128, 1], F32) for m in ("g", "u", "d")}
        onef_t = sb("onef_t", [1, 128], F32)
        onesb_t = sb("onesb_t", [128, 1], BF16)
        e2_t = sb("e2_t", [128, 2], BF16)
        scal_t = sb("scal_t", [1, 8], F32)
        red_t = sb("red_t", [1, 128], F32)
        qsc_t = sb("qsc_t", [128, 1], F32)
        mhb_t = sb("mhb_t", [128, 16], F32)

        q_ld = Sem(top, nc, "q_ld")
        q_st = Sem(top, nc, "q_st")
        q_xb = Sem(top, nc, "q_xb")
        g_ld = Sem(top, nc, "g_ld")
        g_st = Sem(top, nc, "g_st")
        c_v = Sem(top, nc, "c_v")
        c_s = Sem(top, nc, "c_s")
        c_t = Sem(top, nc, "c_t")
        c_g = Sem(top, nc, "c_g")

        mk = {}
        ENGS = ("sync", "gpsimd", "vector", "scalar", "tensor")
        prog = {e: [] for e in ENGS}

        def em(e, fn):
            prog[e].append(fn)

        def run_block():
            import os
            if os.environ.get("KDBG"):
                print(f"BLOCK: q_ld={q_ld.n} q_st={q_st.n} q_xb={q_xb.n} "
                      f"g_ld={g_ld.n} g_st={g_st.n} c_v={c_v.n} c_s={c_s.n} "
                      f"c_t={c_t.n} c_g={c_g.n}")
            with nc.Block() as blk:
                @blk.sync
                def _(eng):
                    for f in prog["sync"]:
                        f(eng)

                @blk.gpsimd
                def _(eng):
                    for f in prog["gpsimd"]:
                        f(eng)

                @blk.vector
                def _(eng):
                    for f in prog["vector"]:
                        f(eng)

                @blk.scalar
                def _(eng):
                    for f in prog["scalar"]:
                        f(eng)

                @blk.tensor
                def _(eng):
                    for f in prog["tensor"]:
                        f(eng)
            for e in ENGS:
                prog[e].clear()

        def snapshot_bar():
            return {c_t.h: c_t.n, c_v.h: c_v.n, c_s.h: c_s.n, c_g.h: c_g.n,
                    q_st.h: q_st.n, g_st.h: g_st.n}

        def wait_bar(eng, bar):
            for h, val in bar.items():
                if val:
                    eng.wait_ge(h, val)

        def bounce_bcast(src_col, dst_tile, wv_v):
            stv = q_st.inc(16)

            def f(eng, wv_v=wv_v, src_col=src_col):
                eng.wait_ge(c_v.h, wv_v)
                eng.dma_start(out=sc8_d[0:1, src_col:src_col + 1],
                              in_=scal_t[0:1, src_col:src_col + 1]
                              ).then_inc(q_st.h, 16)
            em("sync", f)
            ldv = q_ld.inc(16)

            def f(eng, stv=stv, src_col=src_col, dst_tile=dst_tile):
                eng.wait_ge(q_st.h, stv)
                eng.dma_start(
                    out=dst_tile[:],
                    in_=sc8_d[0:1, src_col:src_col + 1].partition_broadcast(128)
                ).then_inc(q_ld.h, 16)
            em("sync", f)
            return ldv

        def bounce_reduce(src_tile, wv, wsem):
            stv = q_st.inc(16)

            def f(eng, wv=wv, wsem=wsem, src_tile=src_tile):
                eng.wait_ge(wsem.h, wv)
                eng.dma_start(out=sc128_d.rearrange("a b -> b a"),
                              in_=src_tile[:]).then_inc(q_st.h, 16)
            em("sync", f)
            ldv = q_ld.inc(16)

            def f(eng, stv=stv):
                eng.wait_ge(q_st.h, stv)
                eng.dma_start(out=red_t[:], in_=sc128_d[:]).then_inc(q_ld.h, 16)
            em("sync", f)
            return ldv

        with ExitStack() as p0:
            psb = lambda n_, s_, d_: p0.enter_context(nc.sbuf_tensor(n_, s_, d_))
            wf_ch = [psb(f"wf_ch{i}", [128, INNER], F32) for i in range(2)]
            csum = psb("csum", [128, 1], F32)
            csum_acc = psb("csum_acc", [128, 1], F32)
            pa_t = psb("pa_t", [128, 2048], F32)
            pp_t = psb("pp_t", [128, 2048], BF16)
            pn_t = psb("pn_t", [128, 2048], BF16)
            tq_t = psb("tq_t", [128, INNER], BF16)

            q_ld.inc(48)

            def f(eng):
                eng.dma_start(out=onef_t[:], in_=onef_d[:]).then_inc(q_ld.h, 16)
                eng.dma_start(out=onesb_t[:], in_=onesb_d[:]).then_inc(q_ld.h, 16)
                eng.dma_start(out=e2_t[:], in_=e2b_d[:]).then_inc(q_ld.h, 16)
            em("sync", f)

            def f(eng):
                eng.memset(qsc_t[:], 1.0 / 32.0)
            em("vector", f)

            wf_lastuse = [None, None]
            for mat, (src_d, dst_d, nch, ncols) in {
                "g": (wgt_d, wgq_d, DC, INNER),
                "u": (wut_d, wuq_d, DC, INNER),
                "d": (wdt_d, wdq_d, IC, D),
            }.items():
                for ch in range(nch):
                    buf = wf_ch[ch % 2]
                    wvr = wf_lastuse[ch % 2]
                    ldv = g_ld.inc(16)

                    def f(eng, wvr=wvr, buf=buf, ch=ch, src_d=src_d, ncols=ncols):
                        if wvr:
                            eng.wait_ge(wvr[0].h, wvr[1])
                        eng.dma_start(out=buf[:, 0:ncols],
                                      in_=src_d[ch * 128:(ch + 1) * 128, :]
                                      ).then_inc(g_ld.h, 16)
                    em("gpsimd", f)
                    vv = c_v.inc(2)

                    def f(eng, ldv=ldv, buf=buf, ch=ch, ncols=ncols, vv=vv):
                        eng.wait_ge(g_ld.h, ldv)
                        eng.tensor_reduce(csum[:], buf[:, 0:ncols], AX, ALU.add,
                                          apply_absolute_value=True).then_inc(c_v.h, 1)
                        eng.wait_ge(c_v.h, vv - 1)
                        if ch == 0:
                            ins = eng.tensor_copy(csum_acc[:], csum[:])
                        else:
                            ins = eng.tensor_add(csum_acc[:], csum_acc[:], csum[:])
                        ins.then_inc(c_v.h, 1)
                    em("vector", f)
                    wf_lastuse[ch % 2] = (c_v, vv)
                ldv = bounce_reduce(csum_acc, c_v.n, c_v)
                nelem = float(nch * 128 * ncols)
                vv = c_v.inc(3)

                def f(eng, ldv=ldv, nelem=nelem, vv=vv):
                    eng.wait_ge(q_ld.h, ldv)
                    eng.tensor_reduce(scal_t[0:1, 0:1], red_t[:], AX, ALU.add).then_inc(c_v.h, 1)
                    eng.wait_ge(c_v.h, vv - 2)
                    eng.tensor_scalar(scal_t[0:1, 1:2], scal_t[0:1, 0:1],
                                      1.0 / nelem, 1e-5, ALU.mult,
                                      ALU.max).then_inc(c_v.h, 1)
                    eng.wait_ge(c_v.h, vv - 1)
                    eng.reciprocal(scal_t[0:1, 2:3], scal_t[0:1, 1:2]).then_inc(c_v.h, 1)
                em("vector", f)
                ld_ws = bounce_bcast(2, wsb[mat], c_v.n)
                ld_mb = bounce_bcast(1, meanb[mat], c_v.n)
                bcv = max(ld_ws, ld_mb)

                for ch in range(nch):
                    buf = wf_ch[ch % 2]
                    wvr = wf_lastuse[ch % 2]
                    ldv = g_ld.inc(16)

                    def f(eng, wvr=wvr, buf=buf, ch=ch, src_d=src_d, ncols=ncols):
                        eng.wait_ge(wvr[0].h, wvr[1])
                        eng.dma_start(out=buf[:, 0:ncols],
                                      in_=src_d[ch * 128:(ch + 1) * 128, :]
                                      ).then_inc(g_ld.h, 16)
                    em("gpsimd", f)
                    stv_prev = g_st.n
                    g_st.inc(16)
                    gv = c_v.inc()

                    def f(eng, ldv=ldv, buf=buf, ch=ch, mat=mat, ncols=ncols,
                          bcv=bcv, stv_prev=stv_prev):
                        eng.wait_ge(g_ld.h, ldv)
                        eng.wait_ge(q_ld.h, bcv)
                        if stv_prev:
                            eng.wait_ge(g_st.h, stv_prev)
                        sw = min(2048, ncols)
                        for sl in range(ncols // sw):
                            cs = slice(sl * sw, (sl + 1) * sw)
                            eng.tensor_scalar(pa_t[:, 0:sw], buf[:, cs],
                                              wsb[mat][:], None, ALU.mult)
                            eng.tensor_scalar(pp_t[:, 0:sw], pa_t[:, 0:sw], 0.5,
                                              None, ALU.is_ge)
                            eng.tensor_scalar(pn_t[:, 0:sw], pa_t[:, 0:sw], -0.5,
                                              None, ALU.is_le)
                            ins = eng.tensor_sub(tq_t[:, cs], pp_t[:, 0:sw],
                                                 pn_t[:, 0:sw])
                        ins.then_inc(c_v.h, 1)
                    em("vector", f)

                    def f(eng, gv=gv, dst_d=dst_d, ch=ch, ncols=ncols):
                        eng.wait_ge(c_v.h, gv)
                        eng.dma_start(out=dst_d[ch * 128:(ch + 1) * 128, :],
                                      in_=tq_t[:, 0:ncols]).then_inc(g_st.h, 16)
                    em("gpsimd", f)
                    wf_lastuse[ch % 2] = (c_v, gv)
                mk[f"p0done_{mat}"] = g_st.n
            run_block()

        def ln_chain(st, src, to_bf16, wv_pre, ln2buf=None):
            wv_pre = dict(wv_pre)
            prev_sq = mk.get(("sqrtc", st - 1), 0)
            if prev_sq:
                wv_pre[c_s.h] = max(wv_pre.get(c_s.h, 0), prev_sq)
            vv = c_v.inc(3)

            def f(eng, src=src, wv_pre=wv_pre, vv=vv):
                for h, val in wv_pre.items():
                    eng.wait_ge(h, val)
                eng.bn_stats(bnb[:, 0:6], src[:, 0:512])
                eng.bn_stats(bnb[:, 6:12], src[:, 512:1024]).then_inc(c_v.h, 1)
                eng.wait_ge(c_v.h, vv - 2)
                eng.bn_aggr(mv[:], bnb[:]).then_inc(c_v.h, 1)
                eng.wait_ge(c_v.h, vv - 1)
                eng.tensor_scalar(ve_t[:], mv[:, 1:2], EPS, None, ALU.add).then_inc(c_v.h, 1)
            em("vector", f)
            sv = c_s.inc()

            def f(eng, vv=vv):
                eng.wait_ge(c_v.h, vv)
                eng.activation(sd_t[:], ve_t[:], ACT.Sqrt).then_inc(c_s.h, 1)
            em("scalar", f)
            mk[("sqrtc", st)] = sv
            vv2 = c_v.inc(2)

            def f(eng, sv=sv, vv2=vv2):
                eng.wait_ge(c_s.h, sv)
                eng.reciprocal(rstd_t[:], sd_t[:]).then_inc(c_v.h, 1)
                eng.wait_ge(c_v.h, vv2 - 1)
                eng.tensor_scalar(nmr_t[:], mv[:, 0:1], rstd_t[:], -1.0,
                                  ALU.mult, ALU.mult).then_inc(c_v.h, 1)
            em("vector", f)
            sv2 = c_v.inc()
            dst = ln2buf if to_bf16 else ln_t

            def f(eng, src=src, dst=dst, vv2=vv2):
                eng.wait_ge(c_v.h, vv2)
                eng.tensor_scalar(dst[:], src[:], rstd_t[:], nmr_t[:],
                                  ALU.mult, ALU.add).then_inc(c_v.h, 1)
            em("vector", f)
            mk[("lnapply", st)] = sv2
            if to_bf16:
                return

            rq = rq_t[st % 2]
            stq_prev = mk.get(("rint_store", st - 2), 0)
            vv3 = c_v.inc(5)

            def f(eng, st=st, rq=rq, stq_prev=stq_prev, vv3=vv3, sv2=sv2):
                if stq_prev:
                    eng.wait_ge(q_st.h, stq_prev)
                eng.wait_ge(c_v.h, sv2)
                eng.tensor_reduce(am_t[:], ln_t[:], AX, ALU.max,
                                  apply_absolute_value=True).then_inc(c_v.h, 1)
                eng.wait_ge(c_v.h, vv3 - 4)
                eng.tensor_scalar(mx_t[:], am_t[:], 1e-5, None,
                                  ALU.max).then_inc(c_v.h, 1)
                eng.wait_ge(c_v.h, vv3 - 3)
                eng.reciprocal(iv_t[:], mx_t[:]).then_inc(c_v.h, 1)
                eng.wait_ge(c_v.h, vv3 - 2)
                eng.tensor_scalar(asc_t[:], iv_t[:], 127.0, None, ALU.mult)
                eng.tensor_scalar(dq_g[:, st:st + 1], mx_t[:], meanb["g"][:],
                                  INV127, ALU.mult, ALU.mult)
                eng.tensor_scalar(dq_u[:, st:st + 1], mx_t[:], meanb["u"][:],
                                  INV127, ALU.mult,
                                  ALU.mult).then_inc(c_v.h, 1)
                eng.wait_ge(c_v.h, vv3 - 1)
                for q in range(2):
                    cs = slice(q * 512, (q + 1) * 512)
                    eng.tensor_scalar(a_t[:], ln_t[:, cs], asc_t[:], 0.5,
                                      ALU.mult, ALU.add)
                    eng.tensor_scalar(ci_t[:], ln_t[:, cs], asc_t[:], 0.5,
                                      ALU.mult, ALU.add)
                    eng.scalar_tensor_tensor(mk_t[:], a_t[:], 0.0, ci_t[:],
                                             ALU.bypass, ALU.is_lt)
                    ins = eng.scalar_tensor_tensor(rq[:, cs], ci_t[:], 0.0, mk_t[:],
                                             ALU.bypass, ALU.subtract)
                ins.then_inc(c_v.h, 1)
            em("vector", f)
            mk[("quant", st)] = vv3

        def ffn_phase(tag, src_res_d, dst_res_d, is_first, is_last, stop_after=None,
                      rint_x=None, guq_x=None):
            rint_l = rint_x if rint_x is not None else rint_d
            guq_l = guq_x if guq_x is not None else guq_d
            entry_bar = snapshot_bar()

            if is_first:
                for st in range(NT):
                    buf = r_tile[0]
                    wvr = mk.get(("lnapply", st - 1), 0)
                    ldv = q_ld.inc(16)

                    def f(eng, st=st, buf=buf, wvr=wvr):
                        if wvr:
                            eng.wait_ge(c_v.h, wvr)
                        eng.dma_start(out=buf[:],
                                      in_=src_res_d[st * 128:(st + 1) * 128, :]
                                      ).then_inc(q_ld.h, 16)
                    em("sync", f)
                    ln_chain(st, buf, False, {q_ld.h: ldv})
                    vq = mk[("quant", st)]
                    stv = q_st.inc(16)

                    def f(eng, st=st, vq=vq):
                        eng.wait_ge(c_v.h, vq)
                        eng.dma_start(out=rint_l[st * 128:(st + 1) * 128, :],
                                      in_=rq_t[st % 2][:]).then_inc(q_st.h, 16)
                    em("sync", f)
                    mk[("rint_store", st)] = stv
                    if st % 4 == 3:
                        mk[("rintg", st // 4)] = stv

            if stop_after == "a":
                run_block()
                return
            c_tiles = NT
            c_parts = "gsupq"
            with ExitStack() as pc:
                psb = lambda n_, s_, d_: pc.enter_context(nc.sbuf_tensor(n_, s_, d_))
                wgq_sb = psb(f"wgq_sb_{tag}", [128, DC, INNER], BF16)
                wuq_sb = psb(f"wuq_sb_{tag}", [128, DC, INNER], BF16)
                xqg = [psb(f"xqg_{tag}{i}", [128, DC, 512], BF16) for i in range(2)]
                sg_t = psb(f"sg_t_{tag}", [128, INNER], F32)
                prod_t = psb(f"prod_t_{tag}", [128, INNER], F32)
                rq2_t = psb(f"rq2_t_{tag}", [128, INNER], BF16)
                gps = [pc.enter_context(nc.psum_tensor(f"gps_{tag}{i}", [128, 2048], F32))
                       for i in range(2)]

                def f(eng, bar=dict(entry_bar)):
                    wait_bar(eng, bar)
                em("tensor", f)

                for w_sb, w_d, key in ((wgq_sb, wgq_d, "g"), (wuq_sb, wuq_d, "u")):
                    wv = mk[f"p0done_{key}"]
                    ldv = g_ld.inc(16 * DC)

                    def f(eng, wv=wv, w_sb=w_sb, w_d=w_d, bar=dict(entry_bar)):
                        wait_bar(eng, bar)
                        eng.wait_ge(g_st.h, wv)
                        for ch in range(DC):
                            eng.dma_start(out=w_sb[:, ch, :],
                                          in_=w_d[ch * 128:(ch + 1) * 128, :]
                                          ).then_inc(g_ld.h, 16)
                    em("gpsimd", f)
                    mk[f"wql_{key}"] = ldv

                for st in range(c_tiles):
                    g = st // 4
                    if st % 4 == 0:
                        wv_store = (mk[("rintg", g)] if is_first
                                    else mk[(f"prevq_{tag}", g)])
                        wv_reuse = mk.get(("xq_lastuse", g - 2), 0)
                        xbv = q_xb.inc(16 * DC)

                        def f(eng, g=g, wv_store=wv_store, wv_reuse=wv_reuse,
                              bar=dict(entry_bar)):
                            wait_bar(eng, bar)
                            eng.wait_ge(q_st.h, wv_store)
                            if wv_reuse:
                                eng.wait_ge(c_t.h, wv_reuse)
                            for ch in range(DC):
                                eng.dma_start_transpose(
                                    out=xqg[g % 2][:, ch, :],
                                    in_=rint_l[g * 512:(g + 1) * 512,
                                               ch * 128:(ch + 1) * 128]
                                ).then_inc(q_xb.h, 16)
                        em("sync", f)
                        mk[("xbarg", g)] = xbv

                    xb = xqg[g % 2]
                    srow = (st % 4) * 128
                    wv_xb = mk[("xbarg", g)]
                    wv_wg = mk["wql_g"]
                    wv_pl = mk.get(("prodlo", st - 1), 0)
                    wv_ph = mk.get(("prodhi", st - 1), 0)
                    if "g" not in c_parts:
                        continue
                    tv = c_t.inc(2)

                    def f(eng, xb=xb, srow=srow, wv_xb=wv_xb, wv_wg=wv_wg,
                          wv_pl=wv_pl, wv_ph=wv_ph):
                        eng.wait_ge(q_xb.h, wv_xb)
                        eng.wait_ge(g_ld.h, wv_wg)
                        if wv_pl:
                            eng.wait_ge(c_v.h, wv_pl)
                        for dc in range(DC):
                            for ob in range(4):
                                ins = eng.matmul(
                                    gps[0][:, ob * 512:(ob + 1) * 512],
                                    xb[:, dc, srow:srow + 128],
                                    wgq_sb[:, dc, ob * 512:(ob + 1) * 512],
                                    start=(dc == 0), stop=(dc == DC - 1))
                        ins.then_inc(c_t.h, 1)
                        if wv_ph:
                            eng.wait_ge(c_v.h, wv_ph)
                        for dc in range(DC):
                            for ob in range(4):
                                ins = eng.matmul(
                                    gps[1][:, ob * 512:(ob + 1) * 512],
                                    xb[:, dc, srow:srow + 128],
                                    wgq_sb[:, dc, 2048 + ob * 512:2048 + (ob + 1) * 512],
                                    start=(dc == 0), stop=(dc == DC - 1))
                        ins.then_inc(c_t.h, 1)
                    em("tensor", f)
                    glo, ghi = tv - 1, tv

                    if "s" not in c_parts:
                        continue
                    wv_q = mk[("quant", st)]
                    sv = c_s.inc(2)

                    def f(eng, st=st, glo=glo, ghi=ghi, wv_q=wv_q,
                          wv_pl2=wv_pl, wv_ph2=wv_ph):
                        eng.wait_ge(c_t.h, glo)
                        eng.wait_ge(c_v.h, wv_q)
                        if wv_pl2:
                            eng.wait_ge(c_v.h, wv_pl2)
                        eng.activation(sg_t[:, 0:2048], gps[0][:], ACT.Silu,
                                       scale=dq_g[:, st:st + 1]).then_inc(c_s.h, 1)
                        eng.wait_ge(c_t.h, ghi)
                        if wv_ph2:
                            eng.wait_ge(c_v.h, wv_ph2)
                        eng.activation(sg_t[:, 2048:4096], gps[1][:], ACT.Silu,
                                       scale=dq_g[:, st:st + 1]).then_inc(c_s.h, 1)
                    em("scalar", f)
                    silo, sihi = sv - 1, sv
                    if "u" not in c_parts:
                        continue

                    wv_wu = mk["wql_u"]
                    tv = c_t.inc(2)

                    def f(eng, xb=xb, srow=srow, silo=silo, sihi=sihi,
                          wv_wu=wv_wu):
                        eng.wait_ge(c_s.h, silo)
                        eng.wait_ge(g_ld.h, wv_wu)
                        for dc in range(DC):
                            for ob in range(4):
                                ins = eng.matmul(
                                    gps[0][:, ob * 512:(ob + 1) * 512],
                                    xb[:, dc, srow:srow + 128],
                                    wuq_sb[:, dc, ob * 512:(ob + 1) * 512],
                                    start=(dc == 0), stop=(dc == DC - 1))
                        ins.then_inc(c_t.h, 1)
                        eng.wait_ge(c_s.h, sihi)
                        for dc in range(DC):
                            for ob in range(4):
                                ins = eng.matmul(
                                    gps[1][:, ob * 512:(ob + 1) * 512],
                                    xb[:, dc, srow:srow + 128],
                                    wuq_sb[:, dc, 2048 + ob * 512:2048 + (ob + 1) * 512],
                                    start=(dc == 0), stop=(dc == DC - 1))
                        ins.then_inc(c_t.h, 1)
                    em("tensor", f)
                    ulo, uhi = tv - 1, tv
                    if st % 4 == 3:
                        mk[("xq_lastuse", g)] = tv

                    if "p" not in c_parts:
                        continue
                    stg_prev = mk.get(("guq_store", st - 1), 0)
                    vv = c_v.inc(7)

                    def f(eng, st=st, ulo=ulo, uhi=uhi, stg_prev=stg_prev, vv=vv):
                        eng.wait_ge(c_t.h, ulo)
                        eng.scalar_tensor_tensor(prod_t[:, 0:2048], gps[0][:],
                                                 dq_u[:, st:st + 1],
                                                 sg_t[:, 0:2048], ALU.mult,
                                                 ALU.mult).then_inc(c_v.h, 1)
                        eng.wait_ge(c_t.h, uhi)
                        eng.scalar_tensor_tensor(prod_t[:, 2048:4096], gps[1][:],
                                                 dq_u[:, st:st + 1],
                                                 sg_t[:, 2048:4096], ALU.mult,
                                                 ALU.mult).then_inc(c_v.h, 1)
                        if stg_prev:
                            eng.wait_ge(q_st.h, stg_prev)
                        eng.wait_ge(c_v.h, vv - 5)
                        eng.tensor_reduce(am_t[:], prod_t[:], AX, ALU.max,
                                          apply_absolute_value=True).then_inc(c_v.h, 1)
                        eng.wait_ge(c_v.h, vv - 4)
                        eng.tensor_scalar(mx_t[:], am_t[:], 1e-5, None,
                                          ALU.max).then_inc(c_v.h, 1)
                        eng.wait_ge(c_v.h, vv - 3)
                        eng.reciprocal(iv_t[:], mx_t[:]).then_inc(c_v.h, 1)
                        eng.wait_ge(c_v.h, vv - 2)
                        eng.tensor_scalar(asc_t[:], iv_t[:], 127.0, None,
                                          ALU.mult)
                        eng.tensor_scalar(dsc_all[:, st:st + 1], mx_t[:],
                                          meanb["d"][:], 0.5 * INV127,
                                          ALU.mult, ALU.mult).then_inc(c_v.h, 1)
                        eng.wait_ge(c_v.h, vv - 1)
                        for q in range(8):
                            cs = slice(q * 512, (q + 1) * 512)
                            eng.tensor_scalar(a_t[:], prod_t[:, cs], asc_t[:],
                                              0.5, ALU.mult, ALU.add)
                            eng.tensor_scalar(ci_t[:], prod_t[:, cs], asc_t[:],
                                              0.5, ALU.mult, ALU.add)
                            eng.scalar_tensor_tensor(mk_t[:], a_t[:], 0.0,
                                                     ci_t[:], ALU.bypass,
                                                     ALU.is_lt)
                            ins = eng.scalar_tensor_tensor(rq2_t[:, cs], ci_t[:], 0.0,
                                                     mk_t[:], ALU.bypass,
                                                     ALU.subtract)
                        ins.then_inc(c_v.h, 1)
                    em("vector", f)
                    mk[("prodlo", st)] = vv - 6
                    mk[("prodhi", st)] = vv - 5
                    mk[("guq_q", st)] = vv

                    if "q" not in c_parts:
                        continue
                    stv = q_st.inc(16)

                    def f(eng, st=st, vv=vv):
                        eng.wait_ge(c_v.h, vv)
                        eng.dma_start(out=guq_l[st * 128:(st + 1) * 128, :],
                                      in_=rq2_t[:]).then_inc(q_st.h, 16)
                    em("sync", f)
                    mk[("guq_store", st)] = stv
                    if st % 4 == 3:
                        mk[("guqg", st // 4)] = stv
                run_block()

            if stop_after == "c":
                return
            d_bar = snapshot_bar()
            with ExitStack() as pd:
                psb = lambda n_, s_, d_: pd.enter_context(nc.sbuf_tensor(n_, s_, d_))
                wdq_sb = psb(f"wdq_sb_{tag}", [128, IC, D], BF16)
                gqT = [psb(f"gqT_{tag}{i}", [128, IC, 512], BF16) for i in range(2)]
                ro_tile = [psb(f"ro_{tag}{i}", [128, D], F32) for i in range(2)]
                rn_tile = [psb(f"rn_{tag}{i}", [128, D], F32) for i in range(2)]
                dps = [pd.enter_context(nc.psum_tensor(f"dps_{tag}{i}", [128, D], F32))
                       for i in range(4)]

                def f(eng, bar=dict(d_bar)):
                    wait_bar(eng, bar)
                em("tensor", f)

                wv = mk["p0done_d"]
                ldv = g_ld.inc(16 * IC)

                def f(eng, wv=wv, bar=dict(d_bar)):
                    wait_bar(eng, bar)
                    eng.wait_ge(g_st.h, wv)
                    for ch in range(IC):
                        eng.dma_start(out=wdq_sb[:, ch, :],
                                      in_=wdq_d[ch * 128:(ch + 1) * 128, :]
                                      ).then_inc(g_ld.h, 16)
                em("gpsimd", f)
                mk["wql_d"] = ldv

                for g in range(NG):
                    wv_store = mk[("guqg", g)]
                    wv_reuse = mk.get(("gqT_lastuse", g - 2), 0)
                    xbv = q_xb.inc(16 * IC)

                    def f(eng, g=g, wv_store=wv_store, wv_reuse=wv_reuse,
                          bar=dict(d_bar)):
                        wait_bar(eng, bar)
                        eng.wait_ge(q_st.h, wv_store)
                        if wv_reuse:
                            eng.wait_ge(c_t.h, wv_reuse)
                        for ch in range(IC):
                            eng.dma_start_transpose(
                                out=gqT[g % 2][:, ch, :],
                                in_=guq_l[g * 512:(g + 1) * 512,
                                          ch * 128:(ch + 1) * 128]
                            ).then_inc(q_xb.h, 16)
                    em("sync", f)

                    wv_wd = mk["wql_d"]
                    wv_epi = mk.get(("depi", (g - 1) * 4 + 3), 0)
                    tv = c_t.inc(4)

                    def f(eng, g=g, xbv=xbv, wv_wd=wv_wd, wv_epi=wv_epi):
                        eng.wait_ge(q_xb.h, xbv)
                        eng.wait_ge(g_ld.h, wv_wd)
                        if wv_epi:
                            eng.wait_ge(c_v.h, wv_epi)
                        for t in range(4):
                            srow = t * 128
                            for ch in range(IC):
                                for ob in range(2):
                                    ins = eng.matmul(
                                        dps[t][:, ob * 512:(ob + 1) * 512],
                                        gqT[g % 2][:, ch, srow:srow + 128],
                                        wdq_sb[:, ch, ob * 512:(ob + 1) * 512],
                                        start=(ch == 0), stop=(ch == IC - 1))
                            ins.then_inc(c_t.h, 1)
                    em("tensor", f)
                    for t in range(4):
                        mk[("down", g * 4 + t)] = tv - 3 + t
                    mk[("gqT_lastuse", g)] = tv

                    for t in range(4):
                        st = g * 4 + t
                        robuf = ro_tile[st % 2]
                        rnbuf = rn_tile[st % 2]
                        wv_ro = mk.get(("depi", st - 2), 0)
                        ldv = q_ld.inc(16)

                        def f(eng, st=st, robuf=robuf, wv_ro=wv_ro):
                            if wv_ro:
                                eng.wait_ge(c_v.h, wv_ro)
                            eng.dma_start(out=robuf[:],
                                          in_=src_res_d[st * 128:(st + 1) * 128, :]
                                          ).then_inc(q_ld.h, 16)
                        em("sync", f)
                        wv_dn = mk[("down", st)]
                        wv_rn = mk.get(("rnstore", st - 2), 0)
                        wv_ap = mk.get(("lnapply", st - 2), 0) if not is_last else 0
                        vv = c_v.inc()

                        def f(eng, st=st, t=t, robuf=robuf, rnbuf=rnbuf,
                              ldv=ldv, wv_dn=wv_dn, wv_rn=wv_rn, wv_ap=wv_ap):
                            eng.wait_ge(q_ld.h, ldv)
                            eng.wait_ge(c_t.h, wv_dn)
                            if wv_rn:
                                eng.wait_ge(q_st.h, wv_rn)
                            if wv_ap:
                                eng.wait_ge(c_v.h, wv_ap)
                            eng.scalar_tensor_tensor(rnbuf[:], dps[t][:],
                                                     dsc_all[:, st:st + 1],
                                                     robuf[:], ALU.mult, ALU.add).then_inc(c_v.h, 1)
                        em("vector", f)
                        mk[("depi", st)] = vv
                        stv = q_st.inc(16)

                        def f(eng, st=st, rnbuf=rnbuf, vv=vv):
                            eng.wait_ge(c_v.h, vv)
                            eng.dma_start(
                                out=dst_res_d[st * 128:(st + 1) * 128, :],
                                in_=rnbuf[:]).then_inc(q_st.h, 16)
                        em("sync", f)
                        mk[("rnstore", st)] = stv

                        if not is_last and tag == "f1":
                            lb = ln2_t[st % 2]
                            pre = {c_v.h: vv}
                            prev_l2 = mk.get(("ln2store", st - 2), 0)
                            if prev_l2:
                                pre[q_st.h] = prev_l2
                            ln_chain(st, rnbuf, True, pre, ln2buf=lb)
                            sv = mk[("lnapply", st)]
                            stv2 = q_st.inc(16)

                            def f(eng, st=st, lb=lb, sv=sv):
                                eng.wait_ge(c_v.h, sv)
                                eng.dma_start(
                                    out=ln2_d[st * 128:(st + 1) * 128, :],
                                    in_=lb[:]).then_inc(q_st.h, 16)
                            em("sync", f)
                            mk[("ln2store", st)] = stv2
                            if st % 4 == 3:
                                mk[("ln2g", st // 4)] = stv2
                run_block()

        def attn_phase(dst_res_d, fuse_ln3):
            entry_bar = snapshot_bar()
            with ExitStack() as p2:
                psb2 = lambda n_, s_, d_: p2.enter_context(nc.sbuf_tensor(n_, s_, d_))
                qfm = psb2("qfm", [128, DC, 2048], BF16)
                kfm = psb2("kfm", [128, DC, 2048], BF16)
                vtm = psb2("vtm", [128, NT, D], BF16)
                bffb = psb2("bffb", [128, D], F32)

                ld_bffb = q_ld.inc(16)

                def f(eng, bar=dict(entry_bar)):
                    wait_bar(eng, bar)
                    eng.dma_start(out=bffb[:],
                                  in_=bff_d.partition_broadcast(128)
                                  ).then_inc(q_ld.h, 16)
                em("sync", f)
                mk["bffb_ld"] = ld_bffb

                with ExitStack() as pf:
                    psb = lambda n_, s_, d_: pf.enter_context(
                        nc.sbuf_tensor(n_, s_, d_))
                    ln2T = psb("ln2T", [128, DC, 2048], BF16)
                    wqb = psb("wqb", [128, DC, D], BF16)
                    wkb = psb("wkb", [128, DC, D], BF16)
                    wvb = psb("wvb", [128, DC, D], BF16)
                    qsq2 = [psb("qsq20", [128, 2048], BF16)]
                    nm2q = psb("nm2q", [2, 8], F32)
                    nm2k = psb("nm2k", [2, 8], F32)
                    m2_t = psb("m2_t", [2, 8], F32)
                    nrow_t = psb("nrow_t", [1, 32], F32)
                    m2r_t = psb("m2r_t", [1, 16], F32)
                    nsc_t = psb("nsc_t", [2, 4], F32)
                    aps = [pf.enter_context(
                        nc.psum_tensor(f"aps{i}", [128, 512], F32))
                        for i in range(8)]

                    def f(eng, bar=dict(entry_bar)):
                        wait_bar(eng, bar)
                    em("tensor", f)

                    ldv = g_ld.inc(16 * DC * 3)

                    def f(eng, bar=dict(entry_bar)):
                        wait_bar(eng, bar)
                        for w_sb, w_d in ((wqb, wqt_d), (wkb, wkt_d),
                                          (wvb, wvt_d)):
                            for ch in range(DC):
                                eng.dma_start(out=w_sb[:, ch, :],
                                              in_=w_d[ch * 128:(ch + 1) * 128, :]
                                              ).then_inc(g_ld.h, 16)
                    em("gpsimd", f)
                    mk["wqkv_ld"] = ldv

                    for g in range(NG):
                        wv_store = mk[("ln2g", g)]
                        xbv = q_xb.inc(16 * DC)

                        def f(eng, g=g, wv_store=wv_store, bar=dict(entry_bar)):
                            wait_bar(eng, bar)
                            eng.wait_ge(q_st.h, wv_store)
                            for ch in range(DC):
                                eng.dma_start_transpose(
                                    out=ln2T[:, ch, g * 512:(g + 1) * 512],
                                    in_=ln2_d[g * 512:(g + 1) * 512,
                                              ch * 128:(ch + 1) * 128]
                                ).then_inc(q_xb.h, 16)
                        em("sync", f)
                        mk[("ln2Tg", g)] = xbv

                    for which, wt, dst in (("q", wqb, qfm), ("k", wkb, kfm)):
                        for oc in range(DC):
                            for nb in range(NG):
                                wv_xb = mk[("ln2Tg", nb)]
                                wv_w = mk["wqkv_ld"]
                                if which == "q":
                                    wv_ev, evsem = mk.get(("qev", oc - 1, nb), 0), c_s
                                elif oc == 0:
                                    wv_ev, evsem = mk[("qev", DC - 1, nb)], c_s
                                else:
                                    wv_ev, evsem = mk.get(("kev", oc - 1, nb), 0), c_v
                                tv = c_t.inc()

                                def f(eng, oc=oc, nb=nb, wt=wt, wv_xb=wv_xb,
                                      wv_w=wv_w, wv_ev=wv_ev, evsem=evsem):
                                    eng.wait_ge(q_xb.h, wv_xb)
                                    eng.wait_ge(g_ld.h, wv_w)
                                    if wv_ev:
                                        eng.wait_ge(evsem.h, wv_ev)
                                    for dc in range(DC):
                                        ins = eng.matmul(
                                            aps[nb][:],
                                            wt[:, dc, oc * 128:(oc + 1) * 128],
                                            ln2T[:, dc, nb * 512:(nb + 1) * 512],
                                            start=(dc == 0), stop=(dc == DC - 1))
                                    ins.then_inc(c_t.h, 1)
                                em("tensor", f)

                                if which == "q":
                                    sv = c_s.inc()

                                    def f(eng, oc=oc, nb=nb, tv=tv, dst=dst):
                                        eng.wait_ge(c_t.h, tv)
                                        eng.activation(
                                            dst[:, oc, nb * 512:(nb + 1) * 512],
                                            aps[nb][:], ACT.Copy,
                                            scale=qsc_t[:]).then_inc(c_s.h, 1)
                                    em("scalar", f)
                                    mk[("qev", oc, nb)] = sv
                                else:
                                    vvv = c_v.inc()

                                    def f(eng, oc=oc, nb=nb, tv=tv, dst=dst):
                                        eng.wait_ge(c_t.h, tv)
                                        eng.tensor_copy(
                                            dst[:, oc, nb * 512:(nb + 1) * 512],
                                            aps[nb][:]).then_inc(c_v.h, 1)
                                    em("vector", f)
                                    mk[("kev", oc, nb)] = vvv

                    for st in range(NT):
                        g = st // 4
                        b0 = 4 + (st % 2) * 2
                        wv_xb = mk[("ln2Tg", g)]
                        wv_ev = mk.get(("vev", st - 2), 0)
                        tv = c_t.inc()

                        def f(eng, st=st, b0=b0, wv_xb=wv_xb, wv_ev=wv_ev):
                            eng.wait_ge(q_xb.h, wv_xb)
                            if wv_ev:
                                eng.wait_ge(c_v.h, wv_ev)
                            for dc in range(DC):
                                for ob in range(2):
                                    ins = eng.matmul(
                                        aps[b0 + ob][:],
                                        ln2T[:, dc, st * 128:(st + 1) * 128],
                                        wvb[:, dc, ob * 512:(ob + 1) * 512],
                                        start=(dc == 0), stop=(dc == DC - 1))
                            ins.then_inc(c_t.h, 1)
                        em("tensor", f)
                        vvv = c_v.inc()

                        def f(eng, st=st, b0=b0, tv=tv):
                            eng.wait_ge(c_t.h, tv)
                            eng.tensor_copy(vtm[:, st, 0:512], aps[b0][:])
                            eng.tensor_copy(vtm[:, st, 512:1024], aps[b0 + 1][:]).then_inc(c_v.h, 1)
                        em("vector", f)
                        mk[("vev", st)] = vvv

                    wv_q = mk[("qev", DC - 1, NG - 1)]
                    wv_k = mk[("kev", DC - 1, NG - 1)]
                    wv_v2 = mk[("vev", NT - 1)]
                    for which, fm, nmsb in (("q", qfm, nm2q), ("k", kfm, nm2k)):
                        for dc in range(DC):
                            sqb = qsq2[0]
                            wv_mm = mk.get((f"nmm_{which}", dc - 1),
                                           mk.get(("nmm_q", DC - 1), 0)
                                           if which == "k" else 0)
                            vv = c_v.inc()

                            def f(eng, fm=fm, dc=dc, sqb=sqb, which=which,
                                  wv_q=wv_q, wv_k=wv_k, wv_v2=wv_v2, wv_mm=wv_mm):
                                if which == "q" and dc == 0:
                                    eng.wait_ge(c_s.h, wv_q)
                                    eng.wait_ge(c_v.h, wv_k)
                                    eng.wait_ge(c_v.h, wv_v2)
                                if wv_mm:
                                    eng.wait_ge(c_t.h, wv_mm)
                                eng.tensor_mul(sqb[:], fm[:, dc, :], fm[:, dc, :]).then_inc(c_v.h, 1)
                            em("vector", f)
                            wv_red = mk.get((f"nred_{which}", dc - 1),
                                            mk.get(("nred_q", DC - 1), 0)
                                            if which == "k" else 0)
                            tv = c_t.inc()

                            def f(eng, vv=vv, sqb=sqb, wv_red=wv_red):
                                eng.wait_ge(c_v.h, vv)
                                if wv_red:
                                    eng.wait_ge(c_v.h, wv_red)
                                for blk in range(4):
                                    ins = eng.matmul(
                                        aps[blk][0:2, :], e2_t[:],
                                        sqb[:, blk * 512:(blk + 1) * 512],
                                        start=True, stop=True)
                                ins.then_inc(c_t.h, 1)
                            em("tensor", f)
                            mk[(f"nmm_{which}", dc)] = tv
                            vv2 = c_v.inc(2)

                            def f(eng, tv=tv, dc=dc, nmsb=nmsb, vv2=vv2):
                                eng.wait_ge(c_t.h, tv)
                                for blk in range(4):
                                    ins = eng.tensor_reduce(
                                        nsc_t[0:2, blk:blk + 1],
                                        aps[blk][0:2, :], AX, ALU.max)
                                ins.then_inc(c_v.h, 1)
                                eng.wait_ge(c_v.h, vv2 - 1)
                                eng.tensor_reduce(nmsb[0:2, dc:dc + 1],
                                                  nsc_t[0:2, 0:4], AX, ALU.max).then_inc(c_v.h, 1)
                            em("vector", f)
                            mk[(f"nred_{which}", dc)] = vv2
                    stv = q_st.inc(32)

                    def f(eng, wv1=mk[("nred_k", DC - 1)],
                          wvq=mk[("nred_q", DC - 1)]):
                        eng.wait_ge(c_v.h, wv1)
                        eng.dma_start(out=scm_d[:],
                                      in_=nm2q[0:2, :]).then_inc(q_st.h, 16)
                        eng.dma_start(out=scmk_d[:],
                                      in_=nm2k[0:2, :]).then_inc(q_st.h, 16)
                    em("sync", f)
                    ldv = q_ld.inc(32)

                    def f(eng, stv=stv):
                        eng.wait_ge(q_st.h, stv)
                        eng.dma_start(out=nrow_t[0:1, 0:16],
                                      in_=scm_d.rearrange("a b -> () (a b)")
                                      ).then_inc(q_ld.h, 16)
                        eng.dma_start(out=nrow_t[0:1, 16:32],
                                      in_=scmk_d.rearrange("a b -> () (a b)")
                                      ).then_inc(q_ld.h, 16)
                    em("sync", f)
                    vv = c_v.inc(2)

                    def f(eng, ldv=ldv, vv=vv):
                        eng.wait_ge(q_ld.h, ldv)
                        eng.tensor_tensor(m2r_t[0:1, :], nrow_t[0:1, 0:16],
                                          nrow_t[0:1, 16:32],
                                          ALU.mult).then_inc(c_v.h, 1)
                        eng.wait_ge(c_v.h, vv - 1)
                        eng.tensor_scalar(m2r_t[0:1, :], m2r_t[0:1, :], 1.0404,
                                          None, ALU.mult).then_inc(c_v.h, 1)
                    em("vector", f)
                    sv = c_s.inc()

                    def f(eng, vv=vv):
                        eng.wait_ge(c_v.h, vv)
                        eng.activation(m2r_t[0:1, :], m2r_t[0:1, :],
                                       ACT.Sqrt).then_inc(c_s.h, 1)
                    em("scalar", f)
                    vv = c_v.inc()

                    def f(eng, sv=sv):
                        eng.wait_ge(c_s.h, sv)
                        eng.tensor_scalar(m2r_t[0:1, :], m2r_t[0:1, :], -1.0,
                                          None, ALU.mult).then_inc(c_v.h, 1)
                    em("vector", f)
                    stv = q_st.inc(16)

                    def f(eng, vv=vv):
                        eng.wait_ge(c_v.h, vv)
                        eng.dma_start(out=scm2_d[:], in_=m2r_t[0:1, 0:16]
                                      ).then_inc(q_st.h, 16)
                    em("sync", f)
                    ldv = q_ld.inc(16 * 16)

                    def f(eng, stv=stv):
                        eng.wait_ge(q_st.h, stv)
                        for h in range(16):
                            i = (h % 2) * 8 + h // 2
                            eng.dma_start(
                                out=mhb_t[:, h:h + 1],
                                in_=scm2_d[0:1, i:i + 1].partition_broadcast(128)
                            ).then_inc(q_ld.h, 16)
                    em("sync", f)
                    mk["mcol_ld"] = ldv
                    run_block()

                ph_bar = snapshot_bar()
                with ExitStack() as ph:
                    psb = lambda n_, s_, d_: ph.enter_context(
                        nc.sbuf_tensor(n_, s_, d_))
                    o1n = psb("o1n", [128, DC, 2048], BF16)
                    wfb = psb("wfb", [128, DC, D], BF16)
                    ro_tile = [psb(f"ro_at{i}", [128, D], F32) for i in range(2)]
                    rn_tile = [psb(f"rn_at{i}", [128, D], F32) for i in range(2)]
                    pt_b = [psb(f"pt_b{i}", [128, 2, 512], BF16)
                            for i in range(3)]
                    lrec = psb("lrec", [1, 1024], F32)
                    lrec2 = psb("lrec2", [1, 1024], F32)
                    rf_t = psb("rf_t", [128, 512], F32)
                    stps = [ph.enter_context(
                        nc.psum_tensor(f"stps{i}", [128, 512], F32))
                        for i in range(2)]
                    o1ps = ph.enter_context(nc.psum_tensor("o1ps", [128, 512],
                                                           F32))
                    o1psb = ph.enter_context(nc.psum_tensor("o1psb", [128, 512],
                                                            F32))
                    laps = ph.enter_context(nc.psum_tensor("laps", [1, 512], F32))
                    lbps = ph.enter_context(nc.psum_tensor("lbps", [1, 512], F32))
                    rps = ph.enter_context(nc.psum_tensor("rps", [128, 512], F32))

                    def f(eng, bar=dict(ph_bar)):
                        wait_bar(eng, bar)
                    em("tensor", f)

                    ldv = g_ld.inc(16 * DC)

                    def f(eng, bar=dict(ph_bar)):
                        wait_bar(eng, bar)
                        for ch in range(DC):
                            eng.dma_start(out=wfb[:, ch, :],
                                          in_=wft_d[ch * 128:(ch + 1) * 128, :]
                                          ).then_inc(g_ld.h, 16)
                    em("gpsimd", f)
                    mk["wff_ld"] = ldv

                    for hp in range(8):
                        for qsb in range(NG):
                            for kst in range(NT):
                                first = (hp == 0 and qsb == 0 and kst == 0)
                                wv_expr = mk.get(("exp", hp, qsb, kst - 1), 0)
                                mcolv = mk["mcol_ld"] if first else 0
                                tv = c_t.inc()

                                def f(eng, hp=hp, qsb=qsb, kst=kst,
                                      wv_expr=wv_expr, mcolv=mcolv):
                                    if mcolv:
                                        eng.wait_ge(q_ld.h, mcolv)
                                    if wv_expr:
                                        eng.wait_ge(c_s.h, wv_expr)
                                    for h01 in range(2):
                                        rr = slice(h01 * 64, h01 * 64 + 64)
                                        ins = eng.matmul(
                                            stps[h01][:],
                                            kfm[rr, hp,
                                                kst * 128:(kst + 1) * 128],
                                            qfm[rr, hp,
                                                qsb * 512:(qsb + 1) * 512],
                                            start=True, stop=True)
                                    ins.then_inc(c_t.h, 1)
                                em("tensor", f)

                                wv_pvr = mk.get(("pv", hp, qsb, kst - 3), 0)
                                sv = c_s.inc()

                                def f(eng, hp=hp, kst=kst, tv=tv,
                                      wv_pvr=wv_pvr):
                                    eng.wait_ge(c_t.h, tv)
                                    if wv_pvr:
                                        eng.wait_ge(c_t.h, wv_pvr)
                                    eng.activation(
                                        pt_b[kst % 3][:, 0, :], stps[0][:],
                                        ACT.Exp,
                                        bias=mhb_t[:, hp * 2:hp * 2 + 1])
                                    eng.activation(
                                        pt_b[kst % 3][:, 1, :], stps[1][:],
                                        ACT.Exp,
                                        bias=mhb_t[:, hp * 2 + 1:hp * 2 + 2]
                                    ).then_inc(c_s.h, 1)
                                em("scalar", f)
                                mk[("exp", hp, qsb, kst)] = sv

                                wv_o1f = 0
                                if kst == 0:
                                    wv_o1f = (mk.get(("o1n", hp, qsb - 1), 0)
                                              if qsb else
                                              mk.get(("o1n", hp - 1, NG - 1), 0))
                                tv = c_t.inc()

                                def f(eng, hp=hp, kst=kst, sv=sv, wv_o1f=wv_o1f):
                                    eng.wait_ge(c_s.h, sv)
                                    if wv_o1f:
                                        eng.wait_ge(c_v.h, wv_o1f)
                                    hh = hp * 2
                                    eng.matmul(o1ps[0:64, :],
                                               vtm[:, kst, hh * 64:(hh + 1) * 64],
                                               pt_b[kst % 3][:, 0, :],
                                               start=(kst == 0),
                                               stop=(kst == NT - 1))
                                    eng.matmul(o1psb[64:128, :],
                                               vtm[:, kst,
                                                   (hh + 1) * 64:(hh + 2) * 64],
                                               pt_b[kst % 3][:, 1, :],
                                               start=(kst == 0),
                                               stop=(kst == NT - 1))
                                    eng.matmul(laps[0:1, :], onesb_t[:],
                                               pt_b[kst % 3][:, 0, :],
                                               start=(kst == 0),
                                               stop=(kst == NT - 1))
                                    eng.matmul(lbps[0:1, :], onesb_t[:],
                                               pt_b[kst % 3][:, 1, :],
                                               start=(kst == 0),
                                               stop=(kst == NT - 1)
                                               ).then_inc(c_t.h, 1)
                                em("tensor", f)
                                mk[("pv", hp, qsb, kst)] = tv

                            wv_pv = mk[("pv", hp, qsb, NT - 1)]
                            vv = c_v.inc(3)

                            def f(eng, wv_pv=wv_pv, vv=vv):
                                eng.wait_ge(c_t.h, wv_pv)
                                eng.tensor_copy(lrec[0:1, 0:512], laps[0:1, :])
                                eng.tensor_copy(lrec[0:1, 512:1024],
                                                lbps[0:1, :]).then_inc(c_v.h, 1)
                                eng.wait_ge(c_v.h, vv - 2)
                                eng.tensor_scalar(lrec[0:1, :], lrec[0:1, :],
                                                  1e-30, None,
                                                  ALU.max).then_inc(c_v.h, 1)
                                eng.wait_ge(c_v.h, vv - 1)
                                eng.reciprocal(lrec2[0:1, 0:512],
                                               lrec[0:1, 0:512])
                                eng.reciprocal(lrec2[0:1, 512:1024],
                                               lrec[0:1, 512:1024]).then_inc(c_v.h, 1)
                            em("vector", f)
                            tv = c_t.inc()

                            def f(eng, vv=vv):
                                eng.wait_ge(c_v.h, vv)
                                eng.matmul(rps[0:64, :], onef_t[0:1, 0:64],
                                           lrec2[0:1, 0:512], start=True,
                                           stop=True)
                                eng.matmul(rps[64:128, :], onef_t[0:1, 0:64],
                                           lrec2[0:1, 512:1024], start=True,
                                           stop=True).then_inc(c_t.h, 1)
                            em("tensor", f)
                            vv = c_v.inc(2)

                            def f(eng, hp=hp, qsb=qsb, tv=tv, vv=vv):
                                eng.wait_ge(c_t.h, tv)
                                eng.tensor_copy(rf_t[:], rps[:]).then_inc(c_v.h, 1)
                                eng.wait_ge(c_v.h, vv - 1)
                                eng.tensor_tensor(
                                    o1n[0:64, hp, qsb * 512:(qsb + 1) * 512],
                                    o1ps[0:64, :], rf_t[0:64, :], ALU.mult)
                                eng.tensor_tensor(
                                    o1n[64:128, hp, qsb * 512:(qsb + 1) * 512],
                                    o1psb[64:128, :], rf_t[64:128, :], ALU.mult).then_inc(c_v.h, 1)
                            em("vector", f)
                            mk[("o1n", hp, qsb)] = vv

                    wv_attn = mk[("o1n", 7, NG - 1)]
                    for st in range(NT):
                        wv_wff = mk["wff_ld"]
                        wv_ep = mk.get(("aepi", st - 1), 0)
                        tv = c_t.inc()

                        def f(eng, st=st, wv_attn=wv_attn,
                              wv_wff=wv_wff, wv_ep=wv_ep):
                            eng.wait_ge(c_v.h, wv_attn)
                            eng.wait_ge(g_ld.h, wv_wff)
                            if wv_ep:
                                eng.wait_ge(c_v.h, wv_ep)
                            for dc in range(DC):
                                for ob in range(2):
                                    ins = eng.matmul(
                                        stps[ob][:],
                                        o1n[:, dc, st * 128:(st + 1) * 128],
                                        wfb[:, dc, ob * 512:(ob + 1) * 512],
                                        start=(dc == 0), stop=(dc == DC - 1))
                            ins.then_inc(c_t.h, 1)
                        em("tensor", f)
                        mk[("proj", st)] = tv

                        robuf = ro_tile[st % 2]
                        rnbuf = rn_tile[st % 2]
                        wv_ro = mk.get(("aepi", st - 2), 0)
                        ldv = q_ld.inc(16)

                        def f(eng, st=st, robuf=robuf, wv_ro=wv_ro):
                            if wv_ro:
                                eng.wait_ge(c_v.h, wv_ro)
                            eng.dma_start(out=robuf[:],
                                          in_=r1_d[st * 128:(st + 1) * 128, :]
                                          ).then_inc(q_ld.h, 16)
                        em("sync", f)
                        wv_rn = mk.get(("rnstore2", st - 2), 0)
                        wv_ap = mk.get(("lnapply", st - 2), 0) if fuse_ln3 else 0
                        wv_bf = mk["bffb_ld"]
                        vv = c_v.inc()

                        def f(eng, st=st, robuf=robuf, rnbuf=rnbuf,
                              ldv=ldv, tv=tv, wv_rn=wv_rn, wv_bf=wv_bf,
                              wv_ap=wv_ap):
                            eng.wait_ge(q_ld.h, ldv)
                            eng.wait_ge(c_t.h, tv)
                            eng.wait_ge(q_ld.h, wv_bf)
                            if wv_rn:
                                eng.wait_ge(q_st.h, wv_rn)
                            if wv_ap:
                                eng.wait_ge(c_v.h, wv_ap)
                            for ob in range(2):
                                cs = slice(ob * 512, (ob + 1) * 512)
                                eng.scalar_tensor_tensor(
                                    rnbuf[:, cs], stps[ob][:], 1.0,
                                    robuf[:, cs], ALU.mult, ALU.add)
                            eng.tensor_add(rnbuf[:], rnbuf[:], bffb[:]).then_inc(c_v.h, 1)
                        em("vector", f)
                        mk[("aepi", st)] = vv
                        stv = q_st.inc(16)

                        def f(eng, st=st, rnbuf=rnbuf, vv=vv):
                            eng.wait_ge(c_v.h, vv)
                            eng.dma_start(
                                out=dst_res_d[st * 128:(st + 1) * 128, :],
                                in_=rnbuf[:]).then_inc(q_st.h, 16)
                        em("sync", f)
                        mk[("rnstore2", st)] = stv

                        if fuse_ln3:
                            ln_chain(st, rnbuf, False, {c_v.h: vv})
                            vq = mk[("quant", st)]
                            stv2 = q_st.inc(16)

                            def f(eng, st=st, vq=vq):
                                eng.wait_ge(c_v.h, vq)
                                eng.dma_start(
                                    out=rint2_d[st * 128:(st + 1) * 128, :],
                                    in_=rq_t[st % 2][:]).then_inc(q_st.h, 16)
                            em("sync", f)
                            mk[("rint_store", st)] = stv2
                            if st % 4 == 3:
                                mk[("prevq_f2", st // 4)] = stv2
                    run_block()

        if n_sublayers == 0:
            pass
        elif n_sublayers == 91:
            ffn_phase("f1", x_d, out_d, is_first=True, is_last=True,
                      stop_after="a")
        elif n_sublayers == 92:
            ffn_phase("f1", x_d, out_d, is_first=True, is_last=True,
                      stop_after="c")
        elif n_sublayers == 1:
            ffn_phase("f1", x_d, out_d, is_first=True, is_last=True)
        elif n_sublayers == 5:
            ffn_phase("f1", x_d, r1_d, is_first=True, is_last=False)
            attn_phase(out_d, fuse_ln3=True)
        else:
            ffn_phase("f1", x_d, r1_d, is_first=True, is_last=False)
            attn_phase(out_d if n_sublayers == 2 else r2_d,
                       fuse_ln3=(n_sublayers >= 3))
            if n_sublayers >= 3:
                mk[("depi", -2)] = mk[("aepi", NT - 2)]
                mk[("depi", -1)] = mk[("aepi", NT - 1)]
                mk[("rnstore", -2)] = mk[("rnstore2", NT - 2)]
                mk[("rnstore", -1)] = mk[("rnstore2", NT - 1)]
                for st in range(NT):
                    mk.pop(("prodlo", st), None)
                    mk.pop(("prodhi", st), None)
                    mk.pop(("guq_store", st), None)
                for g in range(NG):
                    mk.pop(("xq_lastuse", g), None)
                    mk.pop(("gqT_lastuse", g), None)
                ffn_phase("f2", r2_d, out_d, is_first=False, is_last=True,
                          rint_x=rint2_d, guq_x=guq2_d)

        fv = q_st.n

        def f(eng, fv=fv):
            eng.wait_ge(q_st.h, fv)
        em("sync", f)
        run_block()

    return nc


def kernel(L0_attn_x, L0_attn_Wq, L0_attn_Wk, L0_attn_Wv, L0_attn_Wff,
           L0_attn_bff, L0_ffn_w_gate, L0_ffn_w_up, L0_ffn_w_down):
    x = np.ascontiguousarray(np.asarray(L0_attn_x, dtype=np.float32))
    wq = np.ascontiguousarray(np.asarray(L0_attn_Wq, dtype=np.float32).T)
    wk = np.ascontiguousarray(np.asarray(L0_attn_Wk, dtype=np.float32).T)
    wv = np.ascontiguousarray(np.asarray(L0_attn_Wv, dtype=np.float32).T)
    wf = np.ascontiguousarray(np.asarray(L0_attn_Wff, dtype=np.float32).T)
    bf = np.ascontiguousarray(np.asarray(L0_attn_bff, dtype=np.float32)[None, :])
    wg = np.ascontiguousarray(np.asarray(L0_ffn_w_gate, dtype=np.float32).T)
    wu = np.ascontiguousarray(np.asarray(L0_ffn_w_up, dtype=np.float32).T)
    wd = np.ascontiguousarray(np.asarray(L0_ffn_w_down, dtype=np.float32).T)
    onef = np.ones((1, 128), dtype=np.float32)
    onesb = np.ones((128, 1), dtype=ml_dtypes.bfloat16)
    e2b = np.zeros((128, 2), dtype=ml_dtypes.bfloat16)
    e2b[0:64, 0] = 1
    e2b[64:128, 1] = 1

    nc = build_nc(3)
    in_maps = []
    for b in range(8):
        in_maps.append({
            "x": np.ascontiguousarray(x[b]),
            "wqt": wq, "wkt": wk, "wvt": wv, "wft": wf, "bff": bf,
            "wgt": wg, "wut": wu, "wdt": wd,
            "onef": onef, "onesb": onesb, "e2b": e2b,
        })
    res = run_bass_kernel_spmd(nc, in_maps, core_ids=list(range(8)))
    out = np.stack([np.asarray(res.results[b]["out"]) for b in range(8)], axis=0)
    return out.astype(np.float32)
